# version 05
# baseline (speedup 1.0000x reference)
import contextlib
import numpy as np
import concourse.bass as bass
import concourse.mybir as mybir
from concourse.bass_utils import run_bass_kernel_spmd

F32 = mybir.dt.float32
BF16 = mybir.dt.bfloat16
AF = mybir.ActivationFunctionType
ALU = mybir.AluOpType
AX = mybir.AxisListType

NC = 8
S = 16384
D = 2048
R = S // NC
G = 8192
HF = 8192
NT = R // 128
NKD = D // 128
NG = G // 128
NF = HF // 128
NCH = 4
HEADS = 4

C_EPS_RMS, C_EPS_LN, C_NEG_HALF, C_INV512, C_INV2048 = 0, 1, 2, 3, 4


class Sem:
    __slots__ = ("name", "n", "h", "owner")

    def __init__(self, name, owner=None):
        self.name, self.n, self.h, self.owner = name, 0, None, owner


class Rec:
    ENGS = ("sync", "gpsimd", "vector", "scalar", "tensor")

    def __init__(self):
        self.ops = {e: [] for e in self.ENGS}
        self.sems = []

    def sem(self, name, owner=None):
        s = Sem(name, owner)
        self.sems.append(s)
        return s

    def dsems(self, name, k):
        return [self.sem(f"{name}{i}") for i in range(k)]

    def add(self, eng, fn, waits=(), inc=None, amt=1):
        tok = None
        if inc is not None:
            inc.n += amt
            tok = (inc, inc.n)
        self.ops[eng].append((fn, [w for w in waits if w is not None], inc, amt))
        return tok

    def final_dma_waits(self):
        return [(s, s.n) for s in self.sems if s.owner is None and s.n > 0]

    def emit(self, nc, block):
        ctx = contextlib.ExitStack()
        for s in self.sems:
            s.h = ctx.enter_context(nc.semaphore(s.name))

        def make_body(ename):
            def body(eng):
                waited = {}
                for fn, ws, inc, amt in self.ops[ename]:
                    for (s, v) in ws:
                        if s.owner == ename:
                            continue
                        if waited.get(id(s), -1) >= v:
                            continue
                        eng.wait_ge(s.h, v)
                        waited[id(s)] = v
                    inst = fn(eng)
                    if inc is not None:
                        if inc.owner is None:
                            if inst is None:
                                inst = eng.nop()
                            inst.then_inc(inc.h, amt)
                        else:
                            eng.drain().then_inc(inc.h, amt)
            return body

        for ename in self.ENGS:
            if self.ops[ename]:
                getattr(block, ename)(make_body(ename))
        return ctx


def tile_lhsT(W):
    OUT, K = W.shape
    arr = W.reshape(OUT // 128, 128, K // 128, 128)
    return np.ascontiguousarray(arr.transpose(0, 3, 2, 1))


def build_launch_A():
    nc = bass.Bass("TRN2", target_bir_lowering=False)
    x_in = nc.dram_tensor("x", [R, D], F32, kind="ExternalInput").ap()
    WI = nc.dram_tensor("WI", [NKD, 128, NKD, 128], F32, kind="ExternalInput").ap()
    WG = nc.dram_tensor("WG", [NG, 128, NKD, 128], F32, kind="ExternalInput").ap()
    bgT = nc.dram_tensor("bgT", [128, NG], F32, kind="ExternalInput").ap()
    consts = nc.dram_tensor("consts", [128, 8], F32, kind="ExternalInput").ap()

    sp_c = nc.dram_tensor("sp_c", [16, 128, R], BF16, kind="ExternalOutput").ap()
    sp_P = nc.dram_tensor("sp_P", [16, 128, R], BF16, kind="ExternalOutput").ap()
    sp_o = nc.dram_tensor("sp_o", [16, 128, R], BF16, kind="ExternalOutput").ap()
    exch = nc.dram_tensor("exch", [128, 32], F32, kind="ExternalOutput").ap()

    xln_hbm = nc.dram_tensor("xln_hbm", [NKD, R, 128], BF16).ap()

    r = Rec()
    sVE = r.sem("sVE", "vector")
    sSE = r.sem("sSE", "scalar")
    sPE = r.sem("sPE", "tensor")
    s_cst = r.sem("s_cst")
    s_x = r.dsems("s_x", 2)
    s_st = r.dsems("s_st", 2)
    s_tr = r.sem("s_tr")
    s_w = r.dsems("s_w", 2)
    s_spc = r.dsems("s_spc", 2)
    s_spP = r.dsems("s_spP", 2)
    s_spo = r.dsems("s_spo", 2)
    s_ex = r.sem("s_ex")

    st = contextlib.ExitStack()
    A = st.enter_context(nc.sbuf_tensor("A", [128, NKD * R], BF16))
    XP = st.enter_context(nc.sbuf_tensor("XP", [128, NKD * R], BF16))
    pool = [st.enter_context(nc.sbuf_tensor(f"T{i}", [128, R], F32)) for i in range(6)]
    cbuf = [st.enter_context(nc.sbuf_tensor(f"cb{i}", [128, R], BF16)) for i in range(2)]
    pbuf = [st.enter_context(nc.sbuf_tensor(f"pb{i}", [128, R], BF16)) for i in range(2)]
    wbuf = [st.enter_context(nc.sbuf_tensor(f"wb{i}", [128, NKD * 128], BF16)) for i in range(2)]
    cols = st.enter_context(nc.sbuf_tensor("cols", [128, 16], F32))
    cst = st.enter_context(nc.sbuf_tensor("cst", [128, 8], F32))
    bg_sb = st.enter_context(nc.sbuf_tensor("bg_sb", [128, NG], F32))
    exch_sb = st.enter_context(nc.sbuf_tensor("exch_sb", [128, 32], F32))
    wrm = st.enter_context(nc.sbuf_tensor("wrm", [128, 8], F32))
    ps_mm = [st.enter_context(nc.psum_tensor(f"ps{i}", [128, 512], F32)) for i in range(2)]

    r.add("scalar", lambda e: e.memzero(wrm[:, 0:1]))
    r.add("scalar", lambda e: e.memzero(wrm[:, 1:2]))
    r.add("vector", lambda e: e.memset(wrm[:, 2:3], 0.0))
    r.add("gpsimd", lambda e: e.memset(wrm[:, 4:5], 0.0))

    r.add("sync", lambda e: e.dma_start(out=cst[:], in_=consts[:, :]), inc=s_cst, amt=16)
    tok_cst = r.add("sync", lambda e: e.dma_start(out=bg_sb[:], in_=bgT[:, :]), inc=s_cst, amt=16)

    tok_apply = {}
    tok_store = {}
    for tt in range(NT):
        sl = tt % 2
        xb = pool[sl]
        stg = sl * R
        tok_x = r.add("sync", (lambda xb, tt: lambda e: e.dma_start(
            out=xb[:], in_=x_in[tt * 128:(tt + 1) * 128, :]))(xb, tt),
            waits=[tok_apply.get(tt - 2)], inc=s_x[sl], amt=16)
        scr = pool[2]
        r.add("vector", (lambda xb: lambda e: e.reduce_sum(cols[:, 0:1], xb[:], axis=AX.X))(xb),
              waits=[tok_x, tok_cst])
        r.add("vector", (lambda xb, scr: lambda e: e.tensor_mul(scr[:], xb[:], xb[:]))(xb, scr))
        r.add("vector", (lambda scr: lambda e: e.reduce_sum(cols[:, 1:2], scr[:], axis=AX.X))(scr))
        r.add("vector", lambda e: e.tensor_scalar(cols[:, 2:3], cols[:, 0:1], cst[:, C_INV2048:C_INV2048 + 1], None, ALU.mult))
        r.add("vector", lambda e: e.drain())
        r.add("vector", lambda e: e.tensor_mul(cols[:, 3:4], cols[:, 2:3], cols[:, 2:3]))
        r.add("vector", lambda e: e.drain())
        tok_v = r.add("vector", lambda e: e.tensor_scalar(cols[:, 4:5], cols[:, 1:2], cst[:, C_INV2048:C_INV2048 + 1],
                                                          cols[:, 3:4], ALU.mult, ALU.subtract), inc=sVE)
        r.add("scalar", lambda e: e.activation(cols[:, 5:6], cols[:, 4:5], AF.Ln, bias=cst[:, C_EPS_LN:C_EPS_LN + 1]),
              waits=[tok_v])
        r.add("scalar", lambda e: e.drain())
        tok_rstd = r.add("scalar", lambda e: e.activation(cols[:, 5:6], cols[:, 5:6], AF.Exp,
                                                          scale=cst[:, C_NEG_HALF:C_NEG_HALF + 1]), inc=sSE)
        tok_apply[tt] = r.add("vector", (lambda xb, stg: lambda e: e.tensor_scalar(
            A[:, stg:stg + D], xb[:], cols[:, 2:3], cols[:, 5:6], ALU.subtract, ALU.mult))(xb, stg),
            waits=[tok_rstd, tok_store.get(tt - 2)], inc=sVE)
        tok_w = None
        for dt in range(NKD):
            tok_w = r.add("sync", (lambda dt, tt, stg: lambda e: e.dma_start(
                out=xln_hbm[dt, tt * 128:(tt + 1) * 128, :],
                in_=A[:, stg + dt * 128:stg + (dt + 1) * 128]))(dt, tt, stg),
                waits=[tok_apply[tt]], inc=s_st[sl], amt=16)
        tok_store[tt] = tok_w

    tok_tr = None
    for dt in range(NKD):
        tok_tr = r.add("sync", (lambda dt: lambda e: e.dma_start(
            out=A[:, dt * R:(dt + 1) * R], in_=xln_hbm[dt, :, :], transpose=True))(dt),
            waits=[tok_store[NT - 2], tok_store[NT - 1]], inc=s_tr, amt=16)

    ev_tok = {}
    psum_i = [0]
    last_w_user = {}

    def emit_gemm_tile(weight_dram_block, rhs_sbuf, out_cb, extra_mm_waits=(), se_extra=()):
        wslot = (psum_i[0] // NCH) % 2
        wv = wbuf[wslot]
        tok_w = r.add("gpsimd", (lambda wv, weight_dram_block: lambda e: e.dma_start(
            out=wv.ap().rearrange("p (kt m) -> p kt m", m=128), in_=weight_dram_block))(wv, weight_dram_block),
            waits=[last_w_user.get(wslot)], inc=s_w[wslot], amt=16)
        last_e = None
        last_m = None
        for nch in range(NCH):
            i = psum_i[0]
            psum_i[0] += 1
            ps = ps_mm[i % 2]

            def mmgroup(e, wv=wv, ps=ps, nch=nch, rhs_sbuf=rhs_sbuf):
                inst = None
                for kt in range(NKD):
                    inst = e.matmul(ps[:, :], wv[:, kt * 128:(kt + 1) * 128],
                                    rhs_sbuf[:, kt * R + nch * 512: kt * R + nch * 512 + 512],
                                    start=(kt == 0), stop=(kt == NKD - 1))
                return inst
            tok_m = r.add("tensor", mmgroup,
                          waits=[tok_w, ev_tok.get(i - 2)] + list(extra_mm_waits), inc=sPE)
            last_m = tok_m
            eng, fn = out_cb(nch, ps)
            tok_e = r.add(eng, fn, waits=[tok_m] + list(se_extra), inc=(sSE if eng == "scalar" else sVE))
            ev_tok[i] = tok_e
            last_e = tok_e
        last_w_user[wslot] = last_m
        return last_e

    for ht in range(NKD):
        def mk_cb(ht):
            def cb(nch, ps):
                return ("scalar", (lambda ht, nch, ps: lambda e: e.copy(
                    XP[:, ht * R + nch * 512: ht * R + nch * 512 + 512], ps[:, :]))(ht, nch, ps))
            return cb
        emit_gemm_tile(WI[ht, :, :, :], A.ap(), mk_cb(ht), extra_mm_waits=[tok_tr])

    tok_xp_done = ev_tok[NKD * NCH - 1]

    def gate_cb(func, out_ap_fn, gt):
        def cb(nch, ps):
            return ("scalar", (lambda nch, ps: lambda e: e.activation(
                out_ap_fn(nch), ps[:, :], func, bias=bg_sb[:, gt:gt + 1]))(nch, ps))
        return cb

    tok_scan = {}
    tok_u = {}
    tok_spill = {}
    tok_exch = None
    for h in range(HEADS):
        for j in range(4):
            tok_i = emit_gemm_tile(WG[h * 16 + 0 * 4 + j, :, :, :], XP.ap(),
                                   gate_cb(AF.Exp, (lambda j: lambda nch: pool[j][:, nch * 512:(nch + 1) * 512])(j),
                                           h * 16 + 0 * 4 + j),
                                   extra_mm_waits=[tok_xp_done],
                                   se_extra=[tok_scan.get(("c", h - 1, j)), tok_cst])
            tok_ct = emit_gemm_tile(WG[h * 16 + 3 * 4 + j, :, :, :], XP.ap(),
                                    gate_cb(AF.Tanh, lambda nch: pool[4][:, nch * 512:(nch + 1) * 512],
                                            h * 16 + 3 * 4 + j),
                                    se_extra=[tok_u.get((h, j - 1)), tok_scan.get(("P", h - 1, 3))])
            tok_u[(h, j)] = r.add("vector", (lambda j: lambda e: e.tensor_mul(pool[j][:], pool[j][:], pool[4][:]))(j),
                                  waits=[tok_i, tok_ct], inc=sVE)
        tok_z = r.add("vector", lambda e: e.memset(pool[4][:], 0.0), waits=[tok_u[(h, 3)]], inc=sVE)
        for j in range(4):
            ct_idx = h * 4 + j
            w_f = [tok_scan.get(("P", h, j - 1)) if j > 0 else tok_scan.get(("P", h - 1, 3))]
            tok_f = emit_gemm_tile(WG[h * 16 + 1 * 4 + j, :, :, :], XP.ap(),
                                   gate_cb(AF.Sigmoid, lambda nch: pool[5][:, nch * 512:(nch + 1) * 512],
                                           h * 16 + 1 * 4 + j),
                                   se_extra=w_f)
            cb_t = cbuf[j % 2]
            pb_t = pbuf[j % 2]
            tok_c = r.add("vector", (lambda cb_t, j: lambda e: e.tensor_tensor_scan(
                cb_t[:], pool[5][:], pool[j][:], 0.0, ALU.mult, ALU.add))(cb_t, j),
                waits=[tok_f, tok_u[(h, j)], tok_spill.get(("c", ct_idx - 2))], inc=sVE)
            tok_scan[("c", h, j)] = tok_c
            r.add("vector", (lambda cb_t, ct_idx: lambda e: e.tensor_copy(
                exch_sb[:, ct_idx:ct_idx + 1], cb_t[:, R - 1:R]))(cb_t, ct_idx), inc=sVE)
            tok_P = r.add("vector", (lambda pb_t, j: lambda e: e.tensor_tensor_scan(
                pb_t[:], pool[5][:], pool[4][:], 1.0, ALU.mult, ALU.add))(pb_t, j),
                waits=[tok_z, tok_spill.get(("P", ct_idx - 2))], inc=sVE)
            tok_scan[("P", h, j)] = tok_P
            tok_exch = r.add("vector", (lambda pb_t, ct_idx: lambda e: e.tensor_copy(
                exch_sb[:, 16 + ct_idx:17 + ct_idx], pb_t[:, R - 1:R]))(pb_t, ct_idx), inc=sVE)
            tok_spill[("c", ct_idx)] = r.add("scalar", (lambda cb_t, ct_idx: lambda e: e.dma_start(
                out=sp_c[ct_idx, :, :], in_=cb_t[:]))(cb_t, ct_idx), waits=[tok_c], inc=s_spc[j % 2], amt=16)
            tok_spill[("P", ct_idx)] = r.add("scalar", (lambda pb_t, ct_idx: lambda e: e.dma_start(
                out=sp_P[ct_idx, :, :], in_=pb_t[:]))(pb_t, ct_idx), waits=[tok_P], inc=s_spP[j % 2], amt=16)
            ob_off = (ct_idx % 2) * R
            tok_o = emit_gemm_tile(WG[h * 16 + 2 * 4 + j, :, :, :], XP.ap(),
                                   gate_cb(AF.Sigmoid,
                                           (lambda ob_off: lambda nch: A[:, ob_off + nch * 512: ob_off + (nch + 1) * 512])(ob_off),
                                           h * 16 + 2 * 4 + j),
                                   se_extra=[tok_spill.get(("o", ct_idx - 2))])
            tok_spill[("o", ct_idx)] = r.add("scalar", (lambda ob_off, ct_idx: lambda e: e.dma_start(
                out=sp_o[ct_idx, :, :], in_=A[:, ob_off:ob_off + R]))(ob_off, ct_idx),
                waits=[tok_o], inc=s_spo[ct_idx % 2], amt=16)

    r.add("sync", lambda e: e.dma_start(out=exch[:, :], in_=exch_sb[:]),
          waits=[tok_exch], inc=s_ex, amt=16)
    r.add("sync", lambda e: e.nop(), waits=r.final_dma_waits())

    with nc.Block() as block:
        r.emit(nc, block)
    return nc


def build_launch_B():
    nc = bass.Bass("TRN2", target_bir_lowering=False)
    x_in = nc.dram_tensor("x", [R, D], F32, kind="ExternalInput").ap()
    sp_c = nc.dram_tensor("sp_c", [16, 128, R], BF16, kind="ExternalInput").ap()
    sp_P = nc.dram_tensor("sp_P", [16, 128, R], BF16, kind="ExternalInput").ap()
    sp_o = nc.dram_tensor("sp_o", [16, 128, R], BF16, kind="ExternalInput").ap()
    exch_all = nc.dram_tensor("exch_all", [NC * 128, 32], F32, kind="ExternalInput").ap()
    mks = nc.dram_tensor("mks", [128, 8], F32, kind="ExternalInput").ap()
    mks1 = nc.dram_tensor("mks1", [128, 8], F32, kind="ExternalInput").ap()
    rmsT = nc.dram_tensor("rmsT", [128, 4], F32, kind="ExternalInput").ap()
    WO = nc.dram_tensor("WO", [NKD, 128, D], F32, kind="ExternalInput").ap()
    WFG = nc.dram_tensor("WFG", [NF, 128, NKD, 128], F32, kind="ExternalInput").ap()
    WFU = nc.dram_tensor("WFU", [NF, 128, NKD, 128], F32, kind="ExternalInput").ap()
    WD = nc.dram_tensor("WD", [NF, 128, D], F32, kind="ExternalInput").ap()
    consts = nc.dram_tensor("consts", [128, 8], F32, kind="ExternalInput").ap()

    out = nc.dram_tensor("out", [R, D], F32, kind="ExternalOutput").ap()

    x2_hbm = nc.dram_tensor("x2_hbm", [R, D], F32).ap()
    x2ln_hbm = nc.dram_tensor("x2ln_hbm", [NKD, R, 128], BF16).ap()
    gated_hbm = nc.dram_tensor("gated_hbm", [NT, 128, NF, 128], BF16).ap()
    ffnout_hbm = nc.dram_tensor("ffnout_hbm", [R, D], F32).ap()

    r = Rec()
    sVE = r.sem("sVE", "vector")
    sSE = r.sem("sSE", "scalar")
    sPE = r.sem("sPE", "tensor")
    s_cst = r.sem("s_cst")
    s_lc = r.dsems("s_lc", 2)
    s_lp = r.dsems("s_lp", 2)
    s_lo = r.sem("s_lo")
    s_wo = r.sem("s_wo")
    s_x = r.dsems("s_x", 2)
    s_x2w = r.dsems("s_x2w", 2)
    s_x2lnw = r.sem("s_x2lnw")
    s_tr2 = r.sem("s_tr2")
    s_wg = r.dsems("s_wg", 2)
    s_wu = r.dsems("s_wu", 2)
    s_gst = r.dsems("s_gst", 2)
    s_wd = r.sem("s_wd")
    s_gt = r.dsems("s_gt", 2)
    s_ffw = r.dsems("s_ffw", 2)
    s_lf = r.dsems("s_lf", 2)
    s_lx2 = r.dsems("s_lx2", 2)
    s_outw = r.dsems("s_outw", 2)

    st = contextlib.ExitStack()
    A = st.enter_context(nc.sbuf_tensor("A", [128, NKD * R], BF16))
    BIG = st.enter_context(nc.sbuf_tensor("BIG", [128, NKD * R], BF16))
    pool = [st.enter_context(nc.sbuf_tensor(f"T{i}", [128, R], F32)) for i in range(6)]
    cbuf = [st.enter_context(nc.sbuf_tensor(f"cb{i}", [128, R], BF16)) for i in range(2)]
    pbuf = [st.enter_context(nc.sbuf_tensor(f"pb{i}", [128, R], BF16)) for i in range(2)]
    ob0 = st.enter_context(nc.sbuf_tensor("ob0", [128, R], BF16))
    obuf = [ob0, ob0]
    cols = st.enter_context(nc.sbuf_tensor("cols", [128, 16], F32))
    cst = st.enter_context(nc.sbuf_tensor("cst", [128, 8], F32))
    m_sb = st.enter_context(nc.sbuf_tensor("m_sb", [128, 8], F32))
    m1_sb = st.enter_context(nc.sbuf_tensor("m1_sb", [128, 8], F32))
    rms_sb = st.enter_context(nc.sbuf_tensor("rms_sb", [128, 4], F32))
    exa_sb = st.enter_context(nc.sbuf_tensor("exa_sb", [128, NC * 32], F32))
    cin_sb = st.enter_context(nc.sbuf_tensor("cin_sb", [128, 16], F32))
    ta_sb = st.enter_context(nc.sbuf_tensor("ta_sb", [128, 16], F32))
    tb_sb = st.enter_context(nc.sbuf_tensor("tb_sb", [128, 16], F32))
    ones_c = st.enter_context(nc.sbuf_tensor("ones_c", [128, 1], F32))
    ones_r = st.enter_context(nc.sbuf_tensor("ones_r", [1, 128], F32))
    row2 = st.enter_context(nc.sbuf_tensor("row2", [1, D], F32))
    wrm = st.enter_context(nc.sbuf_tensor("wrm", [128, 8], F32))
    PS_A = st.enter_context(nc.psum_tensor("PS_A", [128, D], F32))
    PS_B = st.enter_context(nc.psum_tensor("PS_B", [128, D], F32))

    r.add("scalar", lambda e: e.memzero(wrm[:, 0:1]))
    r.add("scalar", lambda e: e.memzero(wrm[:, 1:2]))
    r.add("vector", lambda e: e.memset(wrm[:, 2:3], 0.0))
    r.add("gpsimd", lambda e: e.memset(wrm[:, 4:5], 0.0))

    r.add("sync", lambda e: e.dma_start(out=cst[:], in_=consts[:, :]), inc=s_cst, amt=16)
    r.add("sync", lambda e: e.dma_start(out=m_sb[:], in_=mks[:, :]), inc=s_cst, amt=16)
    r.add("sync", lambda e: e.dma_start(out=m1_sb[:], in_=mks1[:, :]), inc=s_cst, amt=16)
    r.add("sync", lambda e: e.dma_start(out=rms_sb[:], in_=rmsT[:, :]), inc=s_cst, amt=16)
    tok_cst = r.add("sync", lambda e: e.dma_start(
        out=exa_sb.ap().rearrange("p (r c) -> p r c", c=32),
        in_=exch_all.rearrange("(r p) c -> p r c", p=128)), inc=s_cst, amt=16)
    r.add("vector", lambda e: e.memset(cin_sb[:], 0.0))
    r.add("vector", lambda e: e.memset(ones_c[:], 1.0))
    tok_ones = r.add("vector", lambda e: e.memset(ones_r[:], 1.0), inc=sVE)
    r.add("vector", lambda e: e.nop(), waits=[tok_cst])
    for j in range(NC - 1):
        r.add("vector", (lambda j: lambda e: e.tensor_scalar(
            ta_sb[:], exa_sb[:, j * 32 + 16:j * 32 + 32], m_sb[:, j:j + 1], m1_sb[:, j:j + 1], ALU.mult, ALU.add))(j))
        r.add("vector", (lambda j: lambda e: e.tensor_scalar(
            tb_sb[:], exa_sb[:, j * 32:j * 32 + 16], m_sb[:, j:j + 1], None, ALU.mult))(j))
        r.add("vector", lambda e: e.drain())
        r.add("vector", lambda e: e.tensor_mul(cin_sb[:], cin_sb[:], ta_sb[:]))
        r.add("vector", lambda e: e.drain())
        r.add("vector", lambda e: e.tensor_add(cin_sb[:], cin_sb[:], tb_sb[:]))
        r.add("vector", lambda e: e.drain())
    tok_cin = r.add("vector", lambda e: e.nop(), inc=sVE)

    tok_h = {}
    tok_cn = {}
    tok_fix = {}
    tok_rmsln = None
    prev_psb_reader = None
    for h in range(HEADS):
        toks_sq = {}
        for j in range(4):
            ct_idx = h * 4 + j
            tok_lc = r.add("sync", (lambda cbt, ct_idx: lambda e: e.dma_start(
                out=cbt[:], in_=sp_c[ct_idx, :, :]))(cbuf[j % 2], ct_idx),
                waits=[tok_fix.get(ct_idx - 2)], inc=s_lc[j % 2], amt=16)
            tok_lp = r.add("sync", (lambda pbt, ct_idx: lambda e: e.dma_start(
                out=pbt[:], in_=sp_P[ct_idx, :, :]))(pbuf[j % 2], ct_idx), inc=s_lp[j % 2], amt=16)
            tok_fix[ct_idx] = r.add("vector", (lambda j, ct_idx: lambda e: e.scalar_tensor_tensor(
                pool[j][:], pbuf[j % 2][:], cin_sb[:, ct_idx:ct_idx + 1], cbuf[j % 2][:], ALU.mult, ALU.add))(j, ct_idx),
                waits=[tok_lc, tok_lp, tok_cin, tok_cn.get(ct_idx - 4)], inc=sVE)
            tok_sq = r.add("scalar", (lambda j: lambda e: e.activation(pool[4][:], pool[j][:], AF.Square))(j),
                           waits=[tok_fix[ct_idx], toks_sq.get(j - 1)], inc=sSE)

            def ssq(e, j=j):
                inst = None
                for nch in range(NCH):
                    inst = e.matmul(PS_A[0:1, nch * 512:(nch + 1) * 512], ones_c[:, :],
                                    pool[4][:, nch * 512:(nch + 1) * 512],
                                    start=(j == 0), stop=(j == 3))
                return inst
            toks_sq[j] = r.add("tensor", ssq, waits=[tok_sq, tok_rmsln if j == 0 else None,
                                                      tok_ones if ct_idx == 0 else None], inc=sPE)
        tok_rmsln = r.add("scalar", lambda e: e.activation(row2[:], PS_A[0:1, :], AF.Ln,
                                                           scale=cst[0:1, C_INV512:C_INV512 + 1],
                                                           bias=cst[0:1, C_EPS_RMS:C_EPS_RMS + 1]),
                          waits=[toks_sq[3]], inc=sSE)
        tok_inv = r.add("scalar", lambda e: e.activation(row2[:], row2[:], AF.Exp,
                                                         scale=cst[0:1, C_NEG_HALF:C_NEG_HALF + 1]), inc=sSE)

        def bcast(e):
            inst = None
            for nch in range(NCH):
                inst = e.matmul(PS_B[:, nch * 512:(nch + 1) * 512], ones_r[:, :],
                                row2[0:1, nch * 512:(nch + 1) * 512], start=True, stop=True)
            return inst
        tok_bc = r.add("tensor", bcast, waits=[tok_inv, prev_psb_reader], inc=sPE)
        for j in range(4):
            ct_idx = h * 4 + j
            tok_lo = r.add("sync", (lambda obt, ct_idx: lambda e: e.dma_start(
                out=obt[:], in_=sp_o[ct_idx, :, :]))(obuf[j % 2], ct_idx),
                waits=[tok_h.get(ct_idx - 1)], inc=s_lo, amt=16)
            tok_cn[ct_idx] = r.add("vector", (lambda j: lambda e: e.scalar_tensor_tensor(
                pool[4][:], pool[j][:], rms_sb[:, j:j + 1], PS_B[:, :], ALU.mult, ALU.mult))(j),
                waits=[tok_bc], inc=sVE)
            tok_th = r.add("scalar", lambda e: e.activation(pool[5][:], pool[4][:], AF.Tanh),
                           waits=[tok_cn[ct_idx]], inc=sSE)
            tok_h[ct_idx] = r.add("vector", (lambda ct_idx, j: lambda e: e.tensor_mul(
                A[:, ct_idx * R:(ct_idx + 1) * R], pool[5][:], obuf[j % 2][:]))(ct_idx, j),
                waits=[tok_th, tok_lo], inc=sVE)
        prev_psb_reader = tok_cn[h * 4 + 3]

    tok_wo = None
    for kt in range(NKD):
        tok_wo = r.add("gpsimd", (lambda kt: lambda e: e.dma_start(
            out=BIG[:, kt * R:kt * R + D], in_=WO[kt, :, :]))(kt), inc=s_wo, amt=16)
    tok_att_ev = {}
    tok_x2w = {}
    tok_x2lnw = {}
    tok_x2 = {}
    tok_att = None
    for tt in range(NT):
        ps = PS_A if tt % 2 == 0 else PS_B

        def attmm(e, tt=tt, ps=ps):
            inst = None
            for kt in range(NKD):
                for dch in range(NCH):
                    inst = e.matmul(ps[:, dch * 512:(dch + 1) * 512],
                                    A[:, kt * R + tt * 128: kt * R + (tt + 1) * 128],
                                    BIG[:, kt * R + dch * 512: kt * R + dch * 512 + 512],
                                    start=(kt == 0), stop=(kt == NKD - 1))
            return inst
        first_guard = None
        if tt == 0:
            first_guard = tok_rmsln
        elif tt == 1:
            first_guard = tok_cn[15]
        tok_att = r.add("tensor", attmm,
                        waits=[tok_h[15], tok_wo, tok_att_ev.get(tt - 2), first_guard], inc=sPE)
        att = pool[tt % 2]
        xb = pool[2 + tt % 2]
        x2 = pool[4 + tt % 2]
        tok_att_ev[tt] = r.add("vector", (lambda att, ps: lambda e: e.tensor_copy(att[:], ps[:, :]))(att, ps),
                               waits=[tok_att], inc=sVE)
        tok_x = r.add("sync", (lambda xb, tt: lambda e: e.dma_start(
            out=xb[:], in_=x_in[tt * 128:(tt + 1) * 128, :]))(xb, tt),
            waits=[tok_x2.get(tt - 2)], inc=s_x[tt % 2], amt=16)
        r.add("vector", (lambda att: lambda e: e.reduce_sum(cols[:, 0:1], att[:], axis=AX.X))(att))
        r.add("vector", (lambda att, x2: lambda e: e.tensor_mul(x2[:], att[:], att[:]))(att, x2),
              waits=[tok_x2w.get(tt - 2)])
        r.add("vector", (lambda x2: lambda e: e.reduce_sum(cols[:, 1:2], x2[:], axis=AX.X))(x2))
        r.add("vector", lambda e: e.tensor_scalar(cols[:, 2:3], cols[:, 0:1], cst[:, C_INV2048:C_INV2048 + 1], None, ALU.mult))
        r.add("vector", lambda e: e.drain())
        r.add("vector", lambda e: e.tensor_mul(cols[:, 3:4], cols[:, 2:3], cols[:, 2:3]))
        r.add("vector", lambda e: e.drain())
        tok_v = r.add("vector", lambda e: e.tensor_scalar(cols[:, 4:5], cols[:, 1:2], cst[:, C_INV2048:C_INV2048 + 1],
                                                          cols[:, 3:4], ALU.mult, ALU.subtract), inc=sVE)
        r.add("scalar", lambda e: e.activation(cols[:, 5:6], cols[:, 4:5], AF.Ln, bias=cst[:, C_EPS_LN:C_EPS_LN + 1]),
              waits=[tok_v])
        r.add("scalar", lambda e: e.drain())
        tok_rstd = r.add("scalar", lambda e: e.activation(cols[:, 5:6], cols[:, 5:6], AF.Exp,
                                                          scale=cst[:, C_NEG_HALF:C_NEG_HALF + 1]), inc=sSE)
        r.add("vector", (lambda att: lambda e: e.tensor_scalar(att[:], att[:], cols[:, 2:3], cols[:, 5:6],
                                                               ALU.subtract, ALU.mult))(att), waits=[tok_rstd])
        tok_x2[tt] = r.add("vector", (lambda x2, att, xb: lambda e: e.tensor_add(x2[:], att[:], xb[:]))(x2, att, xb),
                           waits=[tok_x], inc=sVE)
        r.add("vector", (lambda x2: lambda e: e.reduce_sum(cols[:, 8:9], x2[:], axis=AX.X))(x2))
        r.add("vector", (lambda att, x2: lambda e: e.tensor_mul(att[:], x2[:], x2[:]))(att, x2))
        r.add("vector", (lambda att: lambda e: e.reduce_sum(cols[:, 9:10], att[:], axis=AX.X))(att))
        r.add("vector", lambda e: e.tensor_scalar(cols[:, 10:11], cols[:, 8:9], cst[:, C_INV2048:C_INV2048 + 1], None, ALU.mult))
        r.add("vector", lambda e: e.drain())
        r.add("vector", lambda e: e.tensor_mul(cols[:, 11:12], cols[:, 10:11], cols[:, 10:11]))
        r.add("vector", lambda e: e.drain())
        tok_v2 = r.add("vector", lambda e: e.tensor_scalar(cols[:, 12:13], cols[:, 9:10], cst[:, C_INV2048:C_INV2048 + 1],
                                                           cols[:, 11:12], ALU.mult, ALU.subtract), inc=sVE)
        r.add("scalar", lambda e: e.activation(cols[:, 13:14], cols[:, 12:13], AF.Ln, bias=cst[:, C_EPS_LN:C_EPS_LN + 1]),
              waits=[tok_v2])
        r.add("scalar", lambda e: e.drain())
        tok_rstd2 = r.add("scalar", lambda e: e.activation(cols[:, 13:14], cols[:, 13:14], AF.Exp,
                                                           scale=cst[:, C_NEG_HALF:C_NEG_HALF + 1]), inc=sSE)
        tok_x2ln = r.add("vector", (lambda tt, x2: lambda e: e.tensor_scalar(
            obuf[tt % 2][:], x2[:], cols[:, 10:11], cols[:, 13:14], ALU.subtract, ALU.mult))(tt, x2),
            waits=[tok_rstd2, tok_x2lnw.get(tt - 1), tok_h.get(15) if tt == 0 else None], inc=sVE)
        tok_x2w[tt] = r.add("scalar", (lambda x2, tt: lambda e: e.dma_start(
            out=x2_hbm[tt * 128:(tt + 1) * 128, :], in_=x2[:]))(x2, tt), waits=[tok_x2[tt]], inc=s_x2w[tt % 2], amt=16)
        last = None
        for dt in range(NKD):
            last = r.add("scalar", (lambda dt, tt: lambda e: e.dma_start(
                out=x2ln_hbm[dt, tt * 128:(tt + 1) * 128, :], in_=obuf[tt % 2][:, dt * 128:(dt + 1) * 128]))(dt, tt),
                waits=[tok_x2ln], inc=s_x2lnw, amt=16)
        tok_x2lnw[tt] = last

    tok_tr2 = None
    for dt in range(NKD):
        tok_tr2 = r.add("sync", (lambda dt: lambda e: e.dma_start(
            out=A[:, dt * R:(dt + 1) * R], in_=x2ln_hbm[dt, :, :], transpose=True))(dt),
            waits=[tok_x2lnw[NT - 1], tok_att if dt == 0 else None], inc=s_tr2, amt=16)

    wslot = [BIG.ap()[:, i * D:(i + 1) * D] for i in range(4)]
    gslot = [BIG.ap()[:, (4 + i) * D:(5 + i) * D] for i in range(2)]
    tok_pair = {}
    tok_gst = {}
    last_pair = None
    last_silu = None
    for ft in range(NF):
        wg_v = wslot[(ft % 2) * 2 + 0]
        wu_v = wslot[(ft % 2) * 2 + 1]
        tok_wg = r.add("gpsimd", (lambda ft, wg_v: lambda e: e.dma_start(
            out=wg_v.rearrange("p (kt m) -> p kt m", m=128), in_=WFG[ft, :, :, :]))(ft, wg_v),
            waits=[tok_pair.get((ft - 2, 3)), tok_att if ft == 0 else None], inc=s_wg[ft % 2], amt=16)
        tok_wu = r.add("gpsimd", (lambda ft, wu_v: lambda e: e.dma_start(
            out=wu_v.rearrange("p (kt m) -> p kt m", m=128), in_=WFU[ft, :, :, :]))(ft, wu_v),
            inc=s_wu[ft % 2], amt=16)
        gs = gslot[ft % 2]
        for nch in range(NCH):
            slot = (nch % 2) * 2
            psg = PS_A[:, slot * 512:(slot + 1) * 512]
            psu = PS_A[:, (slot + 1) * 512:(slot + 2) * 512]

            def gmm(e, wv=wg_v, psx=psg, nch=nch):
                inst = None
                for kt in range(NKD):
                    inst = e.matmul(psx, wv[:, kt * 128:(kt + 1) * 128],
                                    A[:, kt * R + nch * 512: kt * R + nch * 512 + 512],
                                    start=(kt == 0), stop=(kt == NKD - 1))
                return inst

            def umm(e, wv=wu_v, psx=psu, nch=nch):
                inst = None
                for kt in range(NKD):
                    inst = e.matmul(psx, wv[:, kt * 128:(kt + 1) * 128],
                                    A[:, kt * R + nch * 512: kt * R + nch * 512 + 512],
                                    start=(kt == 0), stop=(kt == NKD - 1))
                return inst
            prev = tok_pair.get((ft - 1, nch + 2) if nch < 2 else (ft, nch - 2))
            guard0 = tok_rmsln if (ft == 0 and nch == 0) else None
            guard1 = tok_att_ev.get(NT - 2) if (ft == 0 and nch == 0) else None
            tok_g = r.add("tensor", gmm, waits=[tok_wg, tok_tr2, prev, guard0, guard1], inc=sPE)
            tok_um = r.add("tensor", umm, waits=[tok_wu], inc=sPE)
            tok_sg = r.add("scalar", (lambda psg: lambda e: e.activation(pool[0][:, 0:512], psg, AF.Silu))(psg),
                           waits=[tok_g, last_pair], inc=sSE)
            last_silu = tok_sg
            tok_pair[(ft, nch)] = r.add("vector", (lambda gs, nch, psu: lambda e: e.tensor_tensor(
                gs[:, nch * 512:(nch + 1) * 512], pool[0][:, 0:512], psu, ALU.mult))(gs, nch, psu),
                waits=[tok_sg, tok_um, tok_gst.get(ft - 2)], inc=sVE)
            last_pair = tok_pair[(ft, nch)]
        last = None
        for tt in range(NT):
            last = r.add("scalar", (lambda tt, ft, gs: lambda e: e.dma_start(
                out=gated_hbm[tt, :, ft, :], in_=gs[:, tt * 128:(tt + 1) * 128]))(tt, ft, gs),
                waits=[tok_pair[(ft, 3)]], inc=s_gst[ft % 2], amt=16)
        tok_gst[ft] = last

    gt_sl = [A.ap()[:, 0:NF * 128], A.ap()[:, NF * 128:2 * NF * 128]]
    tok_dn_ev = {}
    tok_ffw = {}
    last_gt_user = {}
    for dq in range(4):
        tok_wd = None
        for kt in range(NF):
            tok_wd = r.add("gpsimd", (lambda dq, kt: lambda e: e.dma_start(
                out=BIG[:, kt * 512:(kt + 1) * 512], in_=WD[kt, :, dq * 512:(dq + 1) * 512]))(dq, kt),
                waits=[last_pair if (dq == 0 and kt == 0) else None,
                       tok_dn_ev.get((dq - 1, NT - 1)) if kt == 0 else None,
                       tok_dn_ev.get((dq - 1, NT - 2)) if kt == 0 else None], inc=s_wd, amt=16)
        for tt in range(NT):
            gv = gt_sl[tt % 2]
            tok_gtl = r.add("sync", (lambda tt, gv: lambda e: e.dma_start(
                out=gv.rearrange("p (f c) -> p f c", c=128), in_=gated_hbm[tt, :, :, :]))(tt, gv),
                waits=[tok_gst.get(NF - 1) if (dq == 0 and tt == 0) else None,
                       tok_gst.get(NF - 2) if (dq == 0 and tt == 0) else None,
                       last_pair if (dq == 0 and tt < 2) else None,
                       last_gt_user.get(tt % 2)], inc=s_gt[tt % 2], amt=16)
            ps = PS_B[:, (tt % 2) * 512:(tt % 2 + 1) * 512]

            def dnmm(e, gv=gv, psx=ps):
                inst = None
                for kt in range(NF):
                    inst = e.matmul(psx, gv[:, kt * 128:(kt + 1) * 128],
                                    BIG[:, kt * 512:(kt + 1) * 512],
                                    start=(kt == 0), stop=(kt == NF - 1))
                return inst
            tok_dn = r.add("tensor", dnmm,
                           waits=[tok_gtl, tok_wd, tok_dn_ev.get((dq, tt - 2)),
                                  tok_dn_ev.get((dq - 1, NT - 2 + (tt % 2))) if tt < 2 else None],
                           inc=sPE)
            last_gt_user[tt % 2] = tok_dn
            ev = pool[tt % 2]
            tok_ev = r.add("vector", (lambda ev, psx: lambda e: e.tensor_copy(ev[:, 0:512], psx))(ev, ps),
                           waits=[tok_dn, tok_ffw.get((dq, tt - 2)), last_silu if (dq == 0 and tt < 2) else None],
                           inc=sVE)
            tok_dn_ev[(dq, tt)] = tok_ev
            tok_ffw[(dq, tt)] = r.add("scalar", (lambda ev, tt, dq: lambda e: e.dma_start(
                out=ffnout_hbm[tt * 128:(tt + 1) * 128, dq * 512:(dq + 1) * 512], in_=ev[:, 0:512]))(ev, tt, dq),
                waits=[tok_ev], inc=s_ffw[tt % 2], amt=16)

    tok_outw = {}
    tok_ot = {}
    for tt in range(NT):
        ffn = pool[tt % 2]
        x2 = pool[2 + tt % 2]
        ot = pool[4 + tt % 2]
        ffw_done = ([tok_ffw[(dq, tp)] for dq in range(4) for tp in (NT - 2, NT - 1)]
                    if tt == 0 else [])
        tok_lf = r.add("sync", (lambda ffn, tt: lambda e: e.dma_start(
            out=ffn[:], in_=ffnout_hbm[tt * 128:(tt + 1) * 128, :]))(ffn, tt),
            waits=ffw_done + [tok_outw.get(tt - 2),
                              tok_dn_ev.get((3, NT - 2 + (tt % 2))) if tt < 2 else None],
            inc=s_lf[tt % 2], amt=16)
        tok_lx2 = r.add("sync", (lambda x2, tt: lambda e: e.dma_start(
            out=x2[:], in_=x2_hbm[tt * 128:(tt + 1) * 128, :]))(x2, tt), inc=s_lx2[tt % 2], amt=16)
        r.add("vector", (lambda ffn: lambda e: e.reduce_sum(cols[:, 0:1], ffn[:], axis=AX.X))(ffn),
              waits=[tok_lf])
        r.add("vector", (lambda ffn, ot: lambda e: e.tensor_mul(ot[:], ffn[:], ffn[:]))(ffn, ot),
              waits=[tok_outw.get(tt - 2)])
        r.add("vector", (lambda ot: lambda e: e.reduce_sum(cols[:, 1:2], ot[:], axis=AX.X))(ot))
        r.add("vector", lambda e: e.tensor_scalar(cols[:, 2:3], cols[:, 0:1], cst[:, C_INV2048:C_INV2048 + 1], None, ALU.mult))
        r.add("vector", lambda e: e.drain())
        r.add("vector", lambda e: e.tensor_mul(cols[:, 3:4], cols[:, 2:3], cols[:, 2:3]))
        r.add("vector", lambda e: e.drain())
        tok_v = r.add("vector", lambda e: e.tensor_scalar(cols[:, 4:5], cols[:, 1:2], cst[:, C_INV2048:C_INV2048 + 1],
                                                          cols[:, 3:4], ALU.mult, ALU.subtract), inc=sVE)
        r.add("scalar", lambda e: e.activation(cols[:, 5:6], cols[:, 4:5], AF.Ln, bias=cst[:, C_EPS_LN:C_EPS_LN + 1]),
              waits=[tok_v])
        r.add("scalar", lambda e: e.drain())
        tok_rstd = r.add("scalar", lambda e: e.activation(cols[:, 5:6], cols[:, 5:6], AF.Exp,
                                                          scale=cst[:, C_NEG_HALF:C_NEG_HALF + 1]), inc=sSE)
        r.add("vector", (lambda ffn: lambda e: e.tensor_scalar(ffn[:], ffn[:], cols[:, 2:3], cols[:, 5:6],
                                                               ALU.subtract, ALU.mult))(ffn), waits=[tok_rstd])
        tok_ot[tt] = r.add("vector", (lambda ot, ffn, x2: lambda e: e.tensor_add(ot[:], ffn[:], x2[:]))(ot, ffn, x2),
                           waits=[tok_lx2], inc=sVE)
        tok_outw[tt] = r.add("sync", (lambda ot, tt: lambda e: e.dma_start(
            out=out[tt * 128:(tt + 1) * 128, :], in_=ot[:]))(ot, tt), waits=[tok_ot[tt]], inc=s_outw[tt % 2], amt=16)

    r.add("sync", lambda e: e.nop(), waits=r.final_dma_waits())

    with nc.Block() as block:
        r.emit(nc, block)
    return nc


def kernel(L0_attn_x, L0_attn_W_in, L0_attn_W_gate, L0_attn_b_gate, L0_attn_rms_w,
           L0_attn_W_out, L0_ffn_w_gate, L0_ffn_w_up, L0_ffn_w_down):
    x = np.ascontiguousarray(np.asarray(L0_attn_x, np.float32).reshape(S, D))
    WI_t = tile_lhsT(np.asarray(L0_attn_W_in, np.float32))
    WG_t = tile_lhsT(np.asarray(L0_attn_W_gate, np.float32))
    bgT = np.ascontiguousarray(np.asarray(L0_attn_b_gate, np.float32).reshape(NG, 128).T)
    rmsT = np.ascontiguousarray(np.asarray(L0_attn_rms_w, np.float32).reshape(4, 128).T)
    WO_p = np.ascontiguousarray(np.asarray(L0_attn_W_out, np.float32).T.reshape(NKD, 128, D))
    WFG_t = tile_lhsT(np.asarray(L0_ffn_w_gate, np.float32))
    WFU_t = tile_lhsT(np.asarray(L0_ffn_w_up, np.float32))
    WD_p = np.ascontiguousarray(np.asarray(L0_ffn_w_down, np.float32).T.reshape(NF, 128, D))
    consts = np.ascontiguousarray(np.tile(np.array(
        [1e-6, 1e-5, -0.5, 1.0 / 512, 1.0 / 2048, 0.0, 1.0, 0.0], np.float32), (128, 1)))

    ncA = build_launch_A()
    in_A = [{"x": x[k * R:(k + 1) * R], "WI": WI_t, "WG": WG_t, "bgT": bgT, "consts": consts}
            for k in range(NC)]
    resA = run_bass_kernel_spmd(ncA, in_A, core_ids=list(range(NC)))

    exch_all = np.ascontiguousarray(
        np.concatenate([np.asarray(resA.results[k]["exch"]) for k in range(NC)], axis=0))

    ncB = build_launch_B()
    in_B = []
    for k in range(NC):
        m = np.zeros((128, 8), np.float32)
        m[:, :k] = 1.0
        in_B.append({
            "x": x[k * R:(k + 1) * R],
            "sp_c": np.asarray(resA.results[k]["sp_c"]),
            "sp_P": np.asarray(resA.results[k]["sp_P"]),
            "sp_o": np.asarray(resA.results[k]["sp_o"]),
            "exch_all": exch_all,
            "mks": m, "mks1": np.ascontiguousarray(1.0 - m),
            "rmsT": rmsT, "WO": WO_p, "WFG": WFG_t, "WFU": WFU_t, "WD": WD_p,
            "consts": consts,
        })
    resB = run_bass_kernel_spmd(ncB, in_B, core_ids=list(range(NC)))
    out = np.concatenate([np.asarray(resB.results[k]["out"]) for k in range(NC)], axis=0)
    return out.reshape(1, S, D)
